# version 03 (final)
# baseline (speedup 1.0000x reference)
import contextlib
import os
import numpy as np

import concourse.bass as bass
import concourse.mybir as mybir
from concourse.bass_utils import run_bass_kernel_spmd

F32 = mybir.dt.float32
BF16 = mybir.dt.bfloat16
AF = mybir.ActivationFunctionType
ALU = mybir.AluOpType
AX = mybir.AxisListType

P = 128
T = 2048
C = 1024
TT = 16
CC = 8
QG = 4
GT = 4
HP = 8
NH = 16
D = 64
FD = 4096
NE = 8
NFS = 4
FF = 8
MAGIC = 12582912.0
EPS_RMS = 1e-6
EPS_LN = 1e-5
EPS_Q = 1e-5
MBASE = 179168


class Sem:
    def __init__(self, nc, ctx, name):
        self.h = ctx.enter_context(nc.semaphore(name))
        self.n = 0


class Plan:
    def __init__(self):
        self.q = {"sync": [], "gpsimd": [], "tensor": [], "vector": [], "scalar": []}

    def run(self, eng, fn):
        self.q[eng].append(("op", fn))

    def wait(self, eng, dep):
        if dep is None or dep[1] is None:
            return
        self.q[eng].append(("wait", dep[0], dep[1]))

    def inc(self, eng, fn, sem, k=1):
        sem.n += k
        self.q[eng].append(("inc", fn, sem, k))
        return (sem, sem.n)


def build(upto=6, sub=6):
    nc = bass.Bass("TRN2", target_bir_lowering=False, detect_race_conditions=False,
                   dynamic_dma_scratch_size=16384)

    x_d = nc.dram_tensor("x", [T, C], F32, kind="ExternalInput").ap()
    wT_d = {m: nc.dram_tensor(f"w{m}T", [C, C], F32, kind="ExternalInput").ap()
            for m in "qkvo"}
    g_d = {m: nc.dram_tensor(f"g{m}", [P, C], F32, kind="ExternalInput").ap()
           for m in "qkvo"}
    rmsw_d = nc.dram_tensor("rmsw", [P, C], F32, kind="ExternalInput").ap()
    gwT_d = nc.dram_tensor("gwT", [C, NE], F32, kind="ExternalInput").ap()
    W1_d = nc.dram_tensor("W1", [NE, C, FD], F32, kind="ExternalInput").ap()
    W2_d = nc.dram_tensor("W2", [NE, FD, C], F32, kind="ExternalInput").ap()
    b1_d = nc.dram_tensor("b1r", [NE, P, FD // P], F32, kind="ExternalInput").ap()
    b2_d = nc.dram_tensor("b2", [NE, C], F32, kind="ExternalInput").ap()
    identb_d = nc.dram_tensor("identb", [P, P], BF16, kind="ExternalInput").ap()
    ident8_d = nc.dram_tensor("ident8", [8, 8], BF16, kind="ExternalInput").ap()
    triu_d = nc.dram_tensor("triu", [P, P], BF16, kind="ExternalInput").ap()
    oh8_d = nc.dram_tensor("oh8", [NE, 8, P], BF16, kind="ExternalInput").ap()
    iota8_d = nc.dram_tensor("iota8", [P, NE], F32, kind="ExternalInput").ap()
    cvals_d = nc.dram_tensor("cvals", [P, 8], F32, kind="ExternalInput").ap()
    out_d = nc.dram_tensor("out", [T, C], F32, kind="ExternalOutput").ap()
    qsp_d = nc.dram_tensor("qspill", [TT, P, C], BF16, kind="Internal").ap()

    ctx = contextlib.ExitStack()
    with ctx:
        Eb = ctx.enter_context

        def sb(name, shape, dt):
            return Eb(nc.sbuf_tensor("s_" + name, shape, dt))

        def man(name, shape, dt, off, align=32):
            return nc.alloc_sbuf_tensor_at("m_" + name, shape, dt,
                                           offset=MBASE + off, align_bytes=align)

        def pt(name, shape, dt=F32):
            return Eb(nc.psum_tensor(name, shape, dt))

        cst = sb("cst", [P, 8], F32)
        for i, v in enumerate([EPS_RMS, EPS_LN, -0.5, 0.125, 1.0 / C, EPS_Q,
                               1.0 / 127.0, 1.0 / (C * C)]):
            nc.const_aps.register(F32, v, cst[:, i:i + 1])
        identb = sb("identb", [P, P], BF16)
        ident8 = sb("ident8", [8, 8], BF16)
        triu = sb("triu", [P, P], BF16)
        oh8 = [sb(f"oh8_{e}", [8, P], BF16) for e in range(NE)]
        iota8 = sb("iota8", [P, NE], F32)
        gain = sb("gain", [P, C], F32)
        gwc = [sb(f"gwc{c}", [P, NE], BF16) for c in range(CC)]
        b1s = sb("b1s", [P, (FD // P) * NE], F32)
        b2sa = sb("b2sa", [8, 512], BF16)
        b2sb = sb("b2sb", [8, 512], BF16)
        lgt4 = [sb(f"lgt{j}", [8, P], BF16) for j in range(GT)]
        rowb = sb("rowb", [1, P], BF16)
        rowb2 = sb("rowb2", [1, P], BF16)
        totb = sb("totb", [1, 16], BF16)
        totf = sb("totf", [1, 4], F32)

        xz = sb("xz", [P, TT * C], F32)
        tern = sb("tern", [P, CC * C], BF16)
        q_gt = [sb(f"q_{hp}", [P, 512], BF16) for hp in range(HP)]
        k_c = [[sb(f"k_{hp}_{t}", [P, P], BF16) for t in range(TT)] for hp in range(HP)]
        h_c = [[k_c[f][t] for t in range(TT)] for f in range(FF)]
        w1slab = [q_gt[0], q_gt[1]]

        u_b = sb("u_b", [P, C], F32)
        n_b = sb("n_b", [P, C], BF16)
        sqd = sb("sqd", [P, C], BF16)
        nT_c = [sb(f"nT_{c}", [P, P], BF16) for c in range(CC)]
        recb = sb("recb", [P, 8], F32)

        stn = ["ssq", "r_x", "amax", "phi", "ef", "tmp", "s1", "ssz", "m_ln", "r_ln",
               "ssq2", "r_z", "nmx", "sumex", "score", "ssy", "r_y"]
        ST = {n: sb(f"st_{n}", [P, TT], F32) for n in stn}
        colsum = sb("colsum", [P, 8], F32)
        csb = sb("csb", [P, 8], BF16)
        csb2 = sb("csb2", [P, 8], BF16)
        csf = sb("csf", [P, 1], F32)
        SW = {m: sb(f"sw_{m}", [P, 4], F32) for m in "qkvo"}
        lg = sb("lg", [P, NE * TT], F32)
        mx8 = sb("mx8", [P, NE], F32)
        idx8 = sb("idx8", [P, NE], mybir.dt.uint32)
        idxf = sb("idxf", [P, 1], F32)
        smask = sb("smask", [P, NE * TT], F32)

        p4ctx = contextlib.ExitStack()
        E4 = p4ctx.enter_context

        def sb4(name, shape, dt):
            return E4(nc.sbuf_tensor("s_" + name, shape, dt))

        stage = sb4("stage", [P, 4 * C], F32)
        y_g = stage[:]
        ptm = sb4("ptm", [P, C], BF16)
        e_c = [[sb4(f"e_{h}_{i}", [P, 512], BF16) for i in range(2)] for h in range(2)]
        ytb = [sb4(f"ytb_{h}", [P, 512], BF16) for h in range(2)]
        v_c = [[sb4(f"v_{t}_{h}", [P, D + 1], BF16) for h in range(NH)] for t in range(TT)]

        ps_a = pt("ps_a", [P, 512])
        ps_b = pt("ps_b", [P, 512])
        ps_cc = pt("ps_cc", [P, 512])
        ps_d = pt("ps_d", [P, 512])
        ps_e = pt("ps_e", [P, 512])
        ps_f = pt("ps_f", [P, 512])
        ps_t1 = pt("ps_t1", [P, 1024], BF16)
        ps_t2 = pt("ps_t2", [P, 1024], BF16)

        sems = {n: Sem(nc, ctx, n) for n in
                ["dx", "dw", "dg", "dm0", "dm1", "dw2", "sq0", "sq1", "rq0", "rq1",
                 "tv", "vv", "sv", "stm"]}
        dx, dw, dg = sems["dx"], sems["dw"], sems["dg"]
        dm = [sems["dm0"], sems["dm1"]]
        dw2 = sems["dw2"]
        sq = [sems["sq0"], sems["sq1"]]
        rq = [sems["rq0"], sems["rq1"]]
        tv, vv, sv, stm = sems["tv"], sems["vv"], sems["sv"], sems["stm"]

        pl = Plan()
        run, wait, inc = pl.run, pl.wait, pl.inc

        def incw(eng, fn, sem):
            v = inc(eng, fn, sem)
            wait(eng, v)
            return v

        def dma(eng, sem, out_ap, in_ap):
            return inc(eng, lambda e, o=out_ap, i=in_ap: e.dma_start(out=o, in_=i),
                       sem, 16)

        pst_free = {id(ps_t1): None, id(ps_t2): None}
        psum_free = {id(p): None for p in (ps_a, ps_b, ps_cc, ps_d, ps_e, ps_f)}

        dma("sync", dw, cst[:], cvals_d[:])
        dma("sync", dw, identb[:], identb_d[:])
        dma("sync", dw, ident8[:], ident8_d[:])
        dma("sync", dw, triu[:], triu_d[:])
        dma("sync", dw, iota8[:], iota8_d[:])
        for e_i in range(NE):
            dma("sync", dw, oh8[e_i][:], oh8_d[e_i])
        CONSTW = dma("sync", dw, b1s[:].rearrange("p (e f) -> p e f", e=NE),
                     b1_d.transpose([1, 0, 2]))
        XW = None
        for g in range(QG):
            XW = dma("sync", dx,
                     xz[:, g * GT * C:(g + 1) * GT * C].rearrange("p (t c) -> p t c", t=GT),
                     x_d[g * GT * P:(g + 1) * GT * P, :].rearrange("(t p) c -> p t c", p=P))
        GW = None
        for c in range(CC):
            GW = dma("gpsimd", dg, gwc[c][:], gwT_d[c * P:(c + 1) * P, :])
        dma("gpsimd", dg, b2sa[:], b2_d[:, 0:512])
        GW = dma("gpsimd", dg, b2sb[:], b2_d[:, 512:C])
        for t in range(TT):
            for h in range(NH):
                run("gpsimd", lambda e, ap=v_c[t][h][:, D:D + 1]: e.memset(ap, 1.0))
        wait("scalar", CONSTW)

        def prep_tern(m, swc, stage_free, tern_free):
            red_v = [None] * 4
            for hf in range(4):
                wait("sync", stage_free if hf == 0 else red_v[hf - 1])
                src = wT_d[m][hf * 2 * P:(hf + 1) * 2 * P, :].rearrange(
                    "(a p) c -> p a c", p=P)
                w = dma("sync", dw, stage[:, 0:2 * C].rearrange("p (a c) -> p a c", a=2), src)
                wait("vector", w)
                red_v[hf] = inc("vector", lambda e, d=colsum[:, hf:hf + 1]:
                                e.tensor_reduce(d, stage[:, 0:2 * C], AX.X, ALU.add,
                                                apply_absolute_value=True), vv)
            wait("vector", red_v[3])
            incw("vector", lambda e: e.tensor_reduce(
                colsum[:, 4:5], colsum[:, 0:4], AX.X, ALU.add), vv)
            run("vector", lambda e: e.memset(csb[:], 0.0))
            run("vector", lambda e: e.memset(csb2[:], 0.0))
            incw("vector", lambda e: e.tensor_copy(csb[:, 0:1], colsum[:, 4:5]), vv)
            incw("vector", lambda e: e.tensor_copy(csf[:], csb[:, 0:1]), vv)
            incw("vector", lambda e: e.tensor_sub(colsum[:, 6:7], colsum[:, 4:5],
                                                  csf[:]), vv)
            v = inc("vector", lambda e: e.tensor_copy(csb2[:, 0:1], colsum[:, 6:7]), vv)
            wait("vector", v)
            wait("tensor", v)
            wait("tensor", CONSTW)
            wait("tensor", pst_free[id(ps_t1)])
            wait("tensor", pst_free[id(ps_t2)])
            t1 = inc("tensor", lambda e: e.transpose(ps_t1[0:8, 0:P], csb[:],
                                                     identb[:]), tv)
            t1 = inc("tensor", lambda e: e.transpose(ps_t2[0:8, 0:P], csb2[:],
                                                     identb[:]), tv)
            wait("vector", t1)
            incw("vector", lambda e: e.tensor_copy(rowb[:], ps_t1[0:1, 0:P]), vv)
            v = inc("vector", lambda e: e.tensor_copy(rowb2[:], ps_t2[0:1, 0:P]), vv)
            pst_free[id(ps_t1)] = v
            pst_free[id(ps_t2)] = v
            wait("vector", v)
            incw("vector", lambda e: e.tensor_reduce(totf[0:1, 0:1], rowb[:],
                                                     AX.X, ALU.add), vv)
            incw("vector", lambda e: e.tensor_reduce(totf[0:1, 1:2], rowb2[:],
                                                     AX.X, ALU.add), vv)
            incw("vector", lambda e: e.tensor_add(totf[0:1, 0:1], totf[0:1, 0:1],
                                                  totf[0:1, 1:2]), vv)
            incw("vector", lambda e: e.memset(totb[:], 0.0), vv)
            incw("vector", lambda e: e.tensor_copy(totb[0:1, 0:1], totf[0:1, 0:1]), vv)
            incw("vector", lambda e: e.tensor_copy(totf[0:1, 2:3], totb[0:1, 0:1]), vv)
            incw("vector", lambda e: e.tensor_sub(totf[0:1, 3:4], totf[0:1, 0:1],
                                                  totf[0:1, 2:3]), vv)
            v = inc("vector", lambda e: e.tensor_copy(totb[0:1, 8:9],
                                                      totf[0:1, 3:4]), vv)
            wait("vector", v)
            wait("tensor", v)
            wait("tensor", psum_free[id(ps_f)])
            run("tensor", lambda e: e.matmul(ps_f[:, 0:8], oh8[0][0:1, :],
                                             totb[0:1, 0:8], start=True, stop=False))
            t1 = inc("tensor", lambda e: e.matmul(ps_f[:, 0:8], oh8[0][0:1, :],
                                                  totb[0:1, 8:16], start=False,
                                                  stop=True), tv)
            wait("vector", t1)
            incw("vector", lambda e, sw_=swc: e.tensor_scalar(
                sw_[:, 0:1], ps_f[:, 0:1], 1.0 / (C * C), EPS_Q, ALU.mult, ALU.max), vv)
            v = inc("vector", lambda e, sw_=swc: e.reciprocal(sw_[:, 1:2],
                                                              sw_[:, 0:1]), vv)
            psum_free[id(ps_f)] = v
            wait("vector", v)
            incw("vector", lambda e, sw_=swc: e.tensor_scalar(
                sw_[:, 2:3], sw_[:, 0:1], 1.0 / 127.0, None, ALU.mult), vv)
            tern_v = None
            for hf in range(4):
                wait("sync", red_v[3] if hf == 0 else tern_v)
                src = wT_d[m][hf * 2 * P:(hf + 1) * 2 * P, :].rearrange(
                    "(a p) c -> p a c", p=P)
                w = dma("sync", dw, stage[:, 0:2 * C].rearrange("p (a c) -> p a c", a=2), src)
                wait("vector", w)
                if hf == 0:
                    wait("vector", tern_free)
                for cq in range(2):
                    sap = stage[:, cq * C:(cq + 1) * C]
                    dap = tern[:, (hf * 2 + cq) * C:(hf * 2 + cq + 1) * C]
                    run("vector", lambda e, a=sap, sw_=swc: e.tensor_scalar(
                        a, a, sw_[:, 1:2], MAGIC, ALU.mult, ALU.add))
                    run("vector", lambda e, a=sap: e.tensor_scalar(
                        a, a, -MAGIC, 1.0, ALU.add, ALU.min))
                    tern_v = inc("vector", lambda e, a=sap, d=dap: e.tensor_scalar(
                        d, a, -1.0, None, ALU.max), vv)
            return tern_v

        def factor_chain(lo, hi, r_name, compute_r, ssq_name, swc, amax_v, ssq_s):
            sl = slice(lo, hi)
            if compute_r:
                wait("scalar", ssq_s)
                run("scalar", lambda e, a=ST[ssq_name][:, sl], b=ST[r_name][:, sl]:
                    e.activation(b, a, AF.Ln, bias=EPS_RMS, scale=1.0 / C))
                s1 = inc("scalar", lambda e, b=ST[r_name][:, sl]:
                         e.activation(b, b, AF.Exp, scale=-0.5), sv)
                wait("vector", s1)
            wait("vector", amax_v)
            incw("vector", lambda e, a=ST["amax"][:, sl], r=ST[r_name][:, sl],
                 t=ST["tmp"][:, sl]: e.tensor_tensor(t, a, r, ALU.mult), vv)
            incw("vector", lambda e, t=ST["tmp"][:, sl]: e.tensor_scalar(
                t, t, EPS_Q, None, ALU.max), vv)
            incw("vector", lambda e, t=ST["tmp"][:, sl], p=ST["phi"][:, sl]:
                 e.reciprocal(p, t), vv)
            incw("vector", lambda e, p=ST["phi"][:, sl], r=ST[r_name][:, sl]:
                 e.tensor_tensor(p, p, r, ALU.mult), vv)
            incw("vector", lambda e, p=ST["phi"][:, sl]: e.tensor_scalar(
                p, p, 127.0, None, ALU.mult), vv)
            v = incw("vector", lambda e, t=ST["tmp"][:, sl], f=ST["ef"][:, sl], s=swc:
                     e.tensor_scalar(f, t, s[:, 2:3], None, ALU.mult), vv)
            return v

        n_free = None
        nT_free = None
        ptm_free = None

        def quant_round(t, src):
            wait("vector", n_free)
            run("vector", lambda e, s=src: e.tensor_tensor(u_b[:], s, gain[:], ALU.mult))
            run("vector", lambda e, f=ST["phi"][:, t:t + 1]: e.tensor_scalar(
                u_b[:], u_b[:], f, MAGIC, ALU.mult, ALU.add))
            run("vector", lambda e: e.tensor_scalar(u_b[:], u_b[:], -MAGIC, 127.0,
                                                    ALU.add, ALU.min))
            return inc("vector", lambda e: e.tensor_scalar(
                n_b[:], u_b[:], -128.0, None, ALU.max), vv)

        def transpose8(src, dst_aps, src_ready):
            last = None
            for c in range(CC):
                pst = ps_t1 if c % 2 == 0 else ps_t2
                wait("tensor", pst_free[id(pst)])
                if c == 0 and src_ready is not None:
                    wait("tensor", src_ready)
                t_ = inc("tensor", lambda e, a=src[:, c * P:(c + 1) * P], pst=pst:
                         e.transpose(pst[:, 0:P], a, identb[:]), tv)
                wait("vector", t_)
                last = inc("vector", lambda e, d=dst_aps[c], pst=pst:
                           e.tensor_copy(d, pst[:, 0:P]), vv)
                pst_free[id(pst)] = last
            return last

        def projection(psum_pair):
            pa, pb = psum_pair
            wait("tensor", psum_free[id(pa)])
            wait("tensor", psum_free[id(pb)])
            last = None
            for c in range(CC):
                run("tensor", lambda e, w=nT_c[c][:], r=tern[:, c * C:c * C + 512],
                    pa=pa, st=(c == 0), sp=(c == CC - 1):
                    e.matmul(pa[:], w, r, start=st, stop=sp))
                fn = lambda e, w=nT_c[c][:], r=tern[:, c * C + 512:(c + 1) * C],                     pb=pb, st=(c == 0), sp=(c == CC - 1):                     e.matmul(pb[:], w, r, start=st, stop=sp)
                if c < CC - 1:
                    run("tensor", fn)
                else:
                    last = inc("tensor", fn, tv)
            return last

        gain_free = None
        stage_free = None
        tern_free = None
        sq_cnt = [0, 0]
        for m in "qkv"[:max(0, min(3, upto))]:
            swc = SW[m]
            tern_v = prep_tern(m, swc, stage_free, tern_free)
            stage_free = tern_v
            if sub < 2:
                break
            wait("sync", gain_free)
            gw = dma("sync", dw, gain[:], g_d[m])
            wait("vector", gw)
            wait("vector", XW)
            amax_v = None
            ssq_s = None
            for t in range(TT):
                src = xz[:, t * C:(t + 1) * C]
                run("vector", lambda e, s=src: e.tensor_tensor(u_b[:], s, gain[:],
                                                               ALU.mult))
                amax_v = inc("vector", lambda e, d=ST["amax"][:, t:t + 1]:
                             e.reduce_max(d, u_b[:], AX.X, apply_absolute_value=True), vv)
                if m == "q":
                    wait("scalar", XW)
                    ssq_s = inc("scalar", lambda e, s=src, d=ST["ssq"][:, t:t + 1]:
                                e.activation(sqd[:], s, AF.Square, accum_out=d), sv)
            fac_v = factor_chain(0, TT, "r_x", m == "q", "ssq", swc, amax_v, ssq_s)
            if sub < 3:
                break
            t_mm = None
            for t in range(TT):
                par = t % 2
                src = xz[:, t * C:(t + 1) * C]
                n_v = quant_round(t, src)
                nt_v = transpose8(n_b, [nT_c[c][:] for c in range(CC)], n_v)
                n_free = (tv, tv.n)
                if sub < 4:
                    continue
                pp = (ps_a, ps_b) if par == 0 else (ps_cc, ps_d)
                wait("tensor", nt_v)
                wait("tensor", tern_v)
                wait("tensor", nT_free)
                t_mm = projection(pp)
                nT_free = t_mm
                if sub < 5:
                    psum_free[id(pp[0])] = t_mm
                    psum_free[id(pp[1])] = t_mm
                    continue
                wait("vector", t_mm)
                if sub in (9, 10):
                    if sub == 10 or t == 0:
                        wait("vector", ptm_free)
                        ev1 = inc("vector", lambda e, d=sqd[:, 0:512], p=pp[0]:
                                  e.tensor_scalar(d, p[:], 1.0, None, ALU.mult), vv)
                        ev2 = inc("vector", lambda e, d=sqd[:, 512:C], p=pp[1]:
                                  e.tensor_scalar(d, p[:], 1.0, None, ALU.mult), vv)
                        ptm_free = ev2
                        if sub == 9:
                            psum_free[id(pp[0])] = ev1
                            psum_free[id(pp[1])] = ev2
                    if sub == 10:
                        psum_free[id(pp[0])] = t_mm
                        psum_free[id(pp[1])] = t_mm
                    gain_free = ptm_free
                    continue
                if sub in (7, 8):
                    wait("vector", ptm_free)
                    if sub == 7:
                        ev1 = inc("vector", lambda e, d=ptm[:, 0:512], p=pp[0]:
                                  e.tensor_scalar(d, p[:], 1.0, None, ALU.mult), vv)
                        ev2 = inc("vector", lambda e, d=ptm[:, 512:C], p=pp[1]:
                                  e.tensor_scalar(d, p[:], 1.0, None, ALU.mult), vv)
                    else:
                        ev1 = inc("vector", lambda e, d=sqd[:, 0:512], p=pp[0],
                                  f=ST["ef"][:, t:t + 1]:
                                  e.tensor_scalar(d, p[:], f, None, ALU.mult), vv)
                        ev2 = inc("vector", lambda e, d=sqd[:, 512:C], p=pp[1],
                                  f=ST["ef"][:, t:t + 1]:
                                  e.tensor_scalar(d, p[:], f, None, ALU.mult), vv)
                    psum_free[id(pp[0])] = ev1
                    psum_free[id(pp[1])] = ev2
                    ptm_free = ev2
                    gain_free = ev2
                    continue
                if m == "v":
                    last = None
                    for h in range(NH):
                        psrc = pp[0] if h < 8 else pp[1]
                        off = (h % 8) * D
                        last = inc("vector", lambda e, d=v_c[t][h][:, 0:D],
                                   s=psrc[:, off:off + D], f=ST["ef"][:, t:t + 1]:
                                   e.tensor_scalar(d, s, f, None, ALU.mult), vv)
                    psum_free[id(pp[0])] = last
                    psum_free[id(pp[1])] = last
                    gain_free = last
                else:
                    wait("vector", ptm_free)
                    ev1 = inc("vector", lambda e, d=ptm[:, 0:512], s=pp[0][:],
                              f=ST["ef"][:, t:t + 1]:
                              e.tensor_scalar(d, s, f, None, ALU.mult), vv)
                    ev2 = inc("vector", lambda e, d=ptm[:, 512:C], s=pp[1][:],
                              f=ST["ef"][:, t:t + 1]:
                              e.tensor_scalar(d, s, f, None, ALU.mult), vv)
                    psum_free[id(pp[0])] = ev1
                    psum_free[id(pp[1])] = ev2
                    if m == "q":
                        if sub >= 6:
                            wait("sync", ev2)
                            sq_cnt[0] += 1
                            dma("sync", sq[0], qsp_d[t], ptm[:])
                            ptm_free = (sq[0], 16 * sq_cnt[0])
                        else:
                            ptm_free = ev2
                        gain_free = ev2
                    else:
                        tp_v = transpose8(ptm, [k_c[c][t][:] for c in range(CC)],
                                          ev2)
                        ptm_free = (tv, tv.n)
                        gain_free = tp_v
            tern_free = t_mm

        if upto >= 4:
            tern_v_o = prep_tern("o", SW["o"], stage_free, tern_free)
            wait("sync", gain_free)
            gw_o = dma("sync", dw, gain[:], g_d["o"])
            psS = [[ps_a, ps_b], [ps_cc, ps_d]]
            psY = [ps_e, ps_f]
            e_free = [[None, None], [None, None]]
            ytb_free = [None, None]
            y_q_done = None
            stage_y_gate = tern_v_o
            q_g_free = None
            rq_cnt = [0, 0]

            def load_q_group(g):
                nonlocal ptm_free
                wait("sync", q_g_free)
                tdone = None
                for j in range(GT):
                    t = g * GT + j
                    wait("sync", ptm_free)
                    rq_cnt[0] += 1
                    dma("sync", rq[0], ptm[:], qsp_d[t])
                    wait("tensor", (rq[0], 16 * rq_cnt[0]))
                    tdone = transpose8(
                        ptm,
                        [q_gt[c][:, j * P:(j + 1) * P] for c in range(CC)], None)
                    ptm_free = (tv, tv.n)
                return tdone

            for g in range(QG):
                nkt = 4 * g + 4
                y_wr = None
                qld = load_q_group(g)
                for hp in range(HP):
                    for kt in range(nkt):
                        par = kt % 2
                        dep_for_pv = [None, None]
                        for h in range(2):
                            wait("tensor", psum_free[id(psS[h][par])])
                            if hp == 0 and kt == 0:
                                wait("tensor", qld)
                            qk = inc("tensor", lambda e, w=k_c[hp][kt][h * D:(h + 1) * D, :],
                                     r=q_gt[hp][h * D:(h + 1) * D, :],
                                     p=psS[h][par], tp=(h * D, 0):
                                     e.matmul(p[:], w, r, start=True, stop=True,
                                              tile_position=tp), tv)
                            wait("scalar", qk)
                            wait("scalar", e_free[h][par])
                            ex = inc("scalar", lambda e, d=e_c[h][par][:], p=psS[h][par]:
                                     e.activation(d, p[:], AF.Exp, scale=0.125), sv)
                            psum_free[id(psS[h][par])] = ex
                            dep = ex
                            if kt >= 4 * g:
                                j = kt - 4 * g
                                wait("vector", ex)
                                if j > 0:
                                    run("vector", lambda e, a=e_c[h][par][:, 0:j * P]:
                                        e.memset(a, 0.0))
                                dep = inc("vector", lambda e,
                                          a=e_c[h][par][:, (kt - 4 * g) * P:(kt - 4 * g + 1) * P]:
                                          e.tensor_tensor(a, a, triu[:], ALU.mult), vv)
                            dep_for_pv[h] = dep
                        for h in range(2):
                            wait("tensor", dep_for_pv[h])
                            if kt == 0:
                                wait("tensor", psum_free[id(psY[h])])
                            pv = inc("tensor", lambda e, w=v_c[kt][hp * 2 + h][:],
                                     r=e_c[h][par][:], p=psY[h], st=(kt == 0),
                                     sp=(kt == nkt - 1):
                                     e.matmul(p[0:D + 1, :], w, r, start=st, stop=sp), tv)
                            e_free[h][par] = pv
                    ye_s = [None, None]
                    for h in range(2):
                        wait("scalar", e_free[h][0])
                        wait("scalar", e_free[h][1])
                        wait("scalar", ytb_free[h])
                        ye_s[h] = inc("scalar", lambda e, h=h: e.activation(
                            ytb[h][0:D + 1, :], psY[h][0:D + 1, :], AF.Copy), sv)
                        psum_free[id(psY[h])] = ye_s[h]
                    for j in range(GT):
                        for h in range(2):
                            head = hp * 2 + h
                            pst = ps_t1 if h == 0 else ps_t2
                            wait("tensor", ye_s[h])
                            wait("tensor", pst_free[id(pst)])
                            tr = inc("tensor", lambda e, a=ytb[h][:, j * P:(j + 1) * P],
                                     pst=pst: e.transpose(pst[:, 0:P], a, identb[:]), tv)
                            ytb_free[h] = tr
                            wait("vector", tr)
                            rc = inc("vector", lambda e, d=recb[:, j * 2 + h:j * 2 + h + 1],
                                     s=pst[:, D:D + 1]: e.reciprocal(d, s), vv)
                            wait("vector", rc)
                            if y_wr is None:
                                wait("vector", stage_y_gate)
                                wait("vector", y_q_done)
                            y_wr = inc("vector", lambda e,
                                       d=y_g[:, j * C + head * D:j * C + (head + 1) * D],
                                       s=pst[:, 0:D], f=recb[:, j * 2 + h:j * 2 + h + 1]:
                                       e.tensor_scalar(d, s, f, None, ALU.mult), vv)
                            pst_free[id(pst)] = y_wr
                q_g_free = (tv, tv.n)
                wait("vector", gw_o)
                amax_v = None
                ssy_s = None
                for j in range(GT):
                    t = g * GT + j
                    src = y_g[:, j * C:(j + 1) * C]
                    wait("vector", y_wr)
                    run("vector", lambda e, s=src: e.tensor_tensor(u_b[:], s, gain[:],
                                                                   ALU.mult))
                    amax_v = inc("vector", lambda e, d=ST["amax"][:, t:t + 1]:
                                 e.reduce_max(d, u_b[:], AX.X, apply_absolute_value=True), vv)
                    wait("scalar", y_wr)
                    ssy_s = inc("scalar", lambda e, s=src, d=ST["ssy"][:, t:t + 1]:
                                e.activation(sqd[:], s, AF.Square, accum_out=d), sv)
                fac_v = factor_chain(g * GT, (g + 1) * GT, "r_y", True, "ssy",
                                     SW["o"], amax_v, ssy_s)
                s1_v = None
                ssz_s = None
                for j in range(GT):
                    t = g * GT + j
                    par = t % 2
                    src = y_g[:, j * C:(j + 1) * C]
                    n_v = quant_round(t, src)
                    nt_v = transpose8(n_b, [nT_c[c][:] for c in range(CC)], n_v)
                    n_free = (tv, tv.n)
                    pp = (ps_a, ps_b) if par == 0 else (ps_cc, ps_d)
                    wait("tensor", nt_v)
                    wait("tensor", tern_v_o)
                    wait("tensor", nT_free)
                    t_mm = projection(pp)
                    nT_free = t_mm
                    wait("vector", t_mm)
                    run("vector", lambda e, dd=xz[:, t * C:t * C + 512], p=pp[0],
                        f=ST["ef"][:, t:t + 1]: e.scalar_tensor_tensor(
                            dd, p[:], f, dd, ALU.mult, ALU.add))
                    z_v = inc("vector", lambda e, dd=xz[:, t * C + 512:(t + 1) * C], p=pp[1],
                              f=ST["ef"][:, t:t + 1]: e.scalar_tensor_tensor(
                                  dd, p[:], f, dd, ALU.mult, ALU.add), vv)
                    psum_free[id(pp[0])] = z_v
                    psum_free[id(pp[1])] = z_v
                    s1_v = inc("vector", lambda e, s=xz[:, t * C:(t + 1) * C],
                               d=ST["s1"][:, t:t + 1]: e.reduce_sum(d, s, AX.X), vv)
                    wait("scalar", z_v)
                    ssz_s = inc("scalar", lambda e, s=xz[:, t * C:(t + 1) * C],
                                d=ST["ssz"][:, t:t + 1]:
                                e.activation(sqd[:], s, AF.Square, accum_out=d), sv)
                sl = slice(g * GT, (g + 1) * GT)
                wait("vector", ssz_s)
                wait("vector", s1_v)
                incw("vector", lambda e, a=ST["s1"][:, sl], d=ST["m_ln"][:, sl]:
                     e.tensor_scalar(d, a, 1.0 / C, None, ALU.mult), vv)
                incw("vector", lambda e, mm=ST["m_ln"][:, sl], t_=ST["tmp"][:, sl]:
                     e.tensor_tensor(t_, mm, mm, ALU.mult), vv)
                var_v = incw("vector", lambda e, a=ST["ssz"][:, sl], t_=ST["tmp"][:, sl]:
                             e.scalar_tensor_tensor(t_, a, 1.0 / C, t_,
                                                    ALU.mult, ALU.subtract), vv)
                wait("scalar", var_v)
                run("scalar", lambda e, t_=ST["tmp"][:, sl], r=ST["r_ln"][:, sl]:
                    e.activation(r, t_, AF.Ln, bias=EPS_LN))
                rln_s = inc("scalar", lambda e, r=ST["r_ln"][:, sl]:
                            e.activation(r, r, AF.Exp, scale=-0.5), sv)
                wait("vector", rln_s)
                for j in range(GT):
                    t = g * GT + j
                    y_q_done = inc("vector", lambda e, a=xz[:, t * C:(t + 1) * C],
                                   mm=ST["m_ln"][:, t:t + 1], r=ST["r_ln"][:, t:t + 1]:
                                   e.tensor_scalar(a, a, mm, r, ALU.subtract, ALU.mult), vv)

        if upto >= 5:
            p4ctx.close()
            xnT_t = sb("xnT", [P, CC * QG * 512], BF16)

            def xnT(c, g):
                return xnT_t[:, (c * QG + g) * 512:(c * QG + g) * 512 + 512]

            w1c = [[sb(f"w1c_{c}_{f}", [P, P], BF16) for f in range(FF)]
                   for c in range(CC)]
            gsc = sb("gsc", [P, 512], BF16)
            wait("sync", gain_free)
            wait("sync", y_q_done)
            gw_r = dma("sync", dw, gain[:], rmsw_d[:])
            stage_scalar_done = (sv, sv.n)
            ssq2_s = None
            wait("scalar", y_q_done)
            for t in range(TT):
                ssq2_s = inc("scalar", lambda e, s=xz[:, t * C:(t + 1) * C],
                             d=ST["ssq2"][:, t:t + 1]:
                             e.activation(sqd[:], s, AF.Square, accum_out=d), sv)
            wait("scalar", ssq2_s)
            run("scalar", lambda e: e.activation(ST["r_z"][:], ST["ssq2"][:],
                                                 AF.Ln, bias=EPS_RMS, scale=1.0 / C))
            rz_s = inc("scalar", lambda e: e.activation(ST["r_z"][:], ST["r_z"][:],
                                                        AF.Exp, scale=-0.5), sv)
            wait("vector", gw_r)
            wait("vector", rz_s)
            xn_done = None
            for t in range(TT):
                g, j = t // GT, t % GT
                wait("vector", n_free)
                run("vector", lambda e, s=xz[:, t * C:(t + 1) * C]:
                    e.tensor_tensor(u_b[:], s, gain[:], ALU.mult))
                n_v = inc("vector", lambda e, r=ST["r_z"][:, t:t + 1]:
                          e.tensor_scalar(n_b[:], u_b[:], r, None, ALU.mult), vv)
                xn_done = transpose8(n_b,
                                     [xnT(c, g)[:, j * P:(j + 1) * P] for c in range(CC)],
                                     n_v)
                n_free = (tv, tv.n)
            lg_done = None
            wait("tensor", xn_done)
            wait("tensor", GW)
            for g in range(QG):
                wait("tensor", psum_free[id(ps_e)])
                gmm = None
                for c in range(CC):
                    wait("vector", gmm)
                    cp = inc("vector", lambda e, r=xnT(c, g): e.tensor_copy(gsc[:], r), vv)
                    wait("tensor", cp)
                    gmm = inc("tensor", lambda e, w=gwc[c][:, 0:NE],
                              st=(c == 0), sp=(c == CC - 1): e.matmul(
                                  ps_e[0:NE, :], w, gsc[:], start=st, stop=sp), tv)
                gt_ = gmm
                lgt_s = []
                for j in range(GT):
                    wait("scalar", gt_)
                    s_ = inc("scalar", lambda e, j=j: e.activation(
                        lgt4[j][:], ps_e[0:NE, j * P:(j + 1) * P], AF.Copy), sv)
                    lgt_s.append(s_)
                psum_free[id(ps_e)] = lgt_s[-1]
                for j in range(GT):
                    t = g * GT + j
                    wait("tensor", lgt_s[j])
                    wait("tensor", pst_free[id(ps_t1)])
                    tr = inc("tensor", lambda e, j=j: e.transpose(
                        ps_t1[:, 0:NE], lgt4[j][:], ident8[:]), tv)
                    wait("vector", tr)
                    lg_done = inc("vector", lambda e, d=lg[:, t * NE:(t + 1) * NE]:
                                  e.tensor_copy(d, ps_t1[:, 0:NE]), vv)
                    pst_free[id(ps_t1)] = lg_done
            sc_s = None
            wait("vector", lg_done)
            for t in range(TT):
                lgt = lg[:, t * NE:(t + 1) * NE]
                mv = inc("vector", lambda e, lgt=lgt: e.max(mx8[:], lgt), vv)
                wait("vector", mv)
                miv = inc("vector", lambda e, lgt=lgt: e.max_index(idx8[:], mx8[:], lgt), vv)
                wait("vector", miv)
                civ = inc("vector", lambda e: e.tensor_copy(idxf[:], idx8[:, 0:1]), vv)
                run("vector", lambda e, d=ST["nmx"][:, t:t + 1]: e.tensor_scalar(
                    d, mx8[:, 0:1], -1.0, None, ALU.mult))
                wait("vector", civ)
                mk = inc("vector", lambda e, d=smask[:, t * NE:(t + 1) * NE]:
                         e.tensor_scalar(d, iota8[:], idxf[:], None, ALU.is_equal), vv)
                wait("scalar", mk)
                sc_s = inc("scalar", lambda e, lgt=lgt, b=ST["nmx"][:, t:t + 1],
                           d=ST["sumex"][:, t:t + 1]: e.activation(
                               sqd[:, 0:NE], lgt, AF.Exp, bias=b, accum_out=d), sv)
            wait("vector", sc_s)
            sc_v = inc("vector", lambda e: e.reciprocal(ST["score"][:], ST["sumex"][:]), vv)
            wait("vector", sc_v)
            sm_done = None
            for t in range(TT):
                sm_done = inc("vector", lambda e, d=smask[:, t * NE:(t + 1) * NE],
                              f=ST["score"][:, t:t + 1]: e.tensor_scalar(
                                  d, d, f, None, ALU.mult), vv)

            w1_slab_free = [q_g_free, q_g_free]
            w2res_free = nT_free
            w1c_free = stage_scalar_done
            h_free = None
            dm_cnt = [0, 0]
            dw2_cnt = 0
            sil_bank = {0: None, 1: None}
            pd_free = {0: None, 1: None}
            z_done = [None] * TT
            ei_list = [int(q) for q in os.environ.get("KEIL", "").split(",")
                       if q != ""] or list(range(NE))
            fs_list = [int(q) for q in os.environ.get("KFSL", "").split(",")
                       if q != ""] or list(range(NFS))
            for ei in ei_list:
                for fs in fs_list:
                    if os.environ.get("KSTALL"):
                        _sems = {"tv": (tv, tv.n), "vv": (vv, vv.n), "sv": (sv, sv.n),
                                 "dw2": (dw2, 16 * dw2_cnt), "dm0": (dm[0], 16 * dm_cnt[0]),
                                 "dm1": (dm[1], 16 * dm_cnt[1])}
                        for _it in os.environ["KSTALL"].split(","):
                            if _it == "1":
                                for _eng in ("sync", "gpsimd", "tensor", "vector",
                                             "scalar"):
                                    for _s in _sems.values():
                                        wait(_eng, _s)
                            else:
                                _eng, _sn = _it.split(":")
                                wait(_eng, _sems[_sn])
                    f0 = fs * 1024
                    copies_v = None
                    for sl_i in range(16):
                        cc_i, fh = sl_i // 2, sl_i % 2
                        slot = sl_i % 2
                        wait("gpsimd", w1_slab_free[slot])
                        dm_cnt[slot] += 1
                        dma("gpsimd", dm[slot], w1slab[slot][:],
                            W1_d[ei, cc_i * P:(cc_i + 1) * P,
                                 f0 + fh * 512:f0 + fh * 512 + 512])
                        wait("vector", (dm[slot], 16 * dm_cnt[slot]))
                        wait("vector", w1c_free)
                        for k in range(4):
                            copies_v = inc("vector", lambda e, d=w1c[cc_i][fh * 4 + k][:],
                                           s=w1slab[slot][:, k * P:(k + 1) * P]:
                                           e.tensor_copy(d, s), vv)
                        w1_slab_free[slot] = copies_v
                    wait("gpsimd", w2res_free)
                    for ff in range(FF):
                        dw2_cnt += 1
                        dma("gpsimd", dw2, tern[:, ff * C:(ff + 1) * C],
                            W2_d[ei, f0 + ff * P:f0 + (ff + 1) * P, :])
                    wait("tensor", copies_v)
                    if ei == ei_list[0] and fs == 0:
                        wait("tensor", sm_done)
                    h_sil = None
                    it = 0
                    for ff in range(FF):
                        bcol = b1s[:, ei * (FD // P) + fs * FF + ff:
                                   ei * (FD // P) + fs * FF + ff + 1]
                        for g in range(QG):
                            bank = it % 2
                            pa = ps_a if bank == 0 else ps_b
                            wait("tensor", sil_bank[bank])
                            if ei == ei_list[0] and fs == 0 and it < 2:
                                wait("tensor", psum_free[id(pa)])
                            hm = None
                            for cc_i in range(CC):
                                fn = lambda e, w=w1c[cc_i][ff][:], r=xnT(cc_i, g), \
                                    pa=pa, st=(cc_i == 0), sp=(cc_i == CC - 1): \
                                    e.matmul(pa[:], w, r, start=st, stop=sp)
                                if cc_i < CC - 1:
                                    run("tensor", fn)
                                else:
                                    hm = inc("tensor", fn, tv)
                            wait("scalar", hm)
                            wait("scalar", h_free)
                            for j in range(GT):
                                t = g * GT + j
                                h_sil = inc("scalar", lambda e, d=h_c[ff][t][:],
                                            s=pa[:, j * P:(j + 1) * P], b=bcol:
                                            e.activation(d, s, AF.Silu, bias=b), sv)
                            sil_bank[bank] = h_sil
                            it += 1
                    w1c_free = (tv, tv.n)
                    wait("tensor", h_sil)
                    wait("tensor", (dw2, 16 * dw2_cnt))
                    for tt_i in range(TT):
                        par = tt_i % 2
                        pd0 = ps_cc if par == 0 else ps_e
                        pd1 = ps_d if par == 0 else ps_f
                        wait("tensor", pd_free[par])
                        if ei == ei_list[0] and fs == 0 and tt_i < 2:
                            wait("tensor", psum_free[id(pd0)])
                            wait("tensor", psum_free[id(pd1)])
                        if fs == 0:
                            run("tensor", lambda e, w=oh8[ei][:], pd0=pd0:
                                e.matmul(pd0[:], w, b2sa[:], start=True, stop=False))
                            run("tensor", lambda e, w=oh8[ei][:], pd1=pd1:
                                e.matmul(pd1[:], w, b2sb[:], start=True, stop=False))
                        ym = None
                        for ff in range(FF):
                            st_ = (fs != 0 and ff == 0)
                            run("tensor", lambda e, w=h_c[ff][tt_i][:],
                                r=tern[:, ff * C:ff * C + 512], pd0=pd0, st=st_,
                                sp=(ff == FF - 1): e.matmul(pd0[:], w, r, start=st, stop=sp))
                            fn = lambda e, w=h_c[ff][tt_i][:], \
                                r=tern[:, ff * C + 512:(ff + 1) * C], pd1=pd1, st=st_, \
                                sp=(ff == FF - 1): e.matmul(pd1[:], w, r, start=st, stop=sp)
                            if ff < FF - 1:
                                run("tensor", fn)
                            else:
                                ym = inc("tensor", fn, tv)
                        wait("vector", ym)
                        scol = smask[:, tt_i * NE + ei:tt_i * NE + ei + 1]
                        run("vector", lambda e, dd=xz[:, tt_i * C:tt_i * C + 512], p=pd0,
                            f=scol: e.scalar_tensor_tensor(dd, p[:], f, dd,
                                                           ALU.mult, ALU.add))
                        za = inc("vector", lambda e, dd=xz[:, tt_i * C + 512:(tt_i + 1) * C],
                                 p=pd1, f=scol: e.scalar_tensor_tensor(
                                     dd, p[:], f, dd, ALU.mult, ALU.add), vv)
                        pd_free[par] = za
                        kd = os.environ.get("KDBG", "")
                        if kd and [int(q) for q in kd.split(",")] == [ei, fs, tt_i]:
                            run("vector", lambda e, p=pd0: e.tensor_copy(
                                u_b[:, 0:512], p[:]))
                            run("vector", lambda e, p=pd1: e.tensor_copy(
                                u_b[:, 512:C], p[:]))
                            dbg_v = inc("vector", lambda e, f=scol: e.tensor_copy(
                                recb[:, 0:1], f), vv)
                            pd_free[par] = dbg_v
                        if ei == ei_list[-1] and fs == fs_list[-1]:
                            z_done[tt_i] = za
                    h_free = (tv, tv.n)
                    w2res_free = (tv, tv.n)

        if upto >= 6:
            s1_v = None
            ssz_s = None
            for t in range(TT):
                wait("vector", z_done[t])
                s1_v = inc("vector", lambda e, s=xz[:, t * C:(t + 1) * C],
                           d=ST["s1"][:, t:t + 1]: e.reduce_sum(d, s, AX.X), vv)
                wait("scalar", z_done[t])
                ssz_s = inc("scalar", lambda e, s=xz[:, t * C:(t + 1) * C],
                            d=ST["ssz"][:, t:t + 1]:
                            e.activation(sqd[:], s, AF.Square, accum_out=d), sv)
            wait("vector", ssz_s)
            wait("vector", s1_v)
            incw("vector", lambda e: e.tensor_scalar(ST["m_ln"][:], ST["s1"][:],
                                                     1.0 / C, None, ALU.mult), vv)
            incw("vector", lambda e: e.tensor_tensor(ST["tmp"][:], ST["m_ln"][:],
                                                     ST["m_ln"][:], ALU.mult), vv)
            var_v = incw("vector", lambda e: e.scalar_tensor_tensor(
                ST["tmp"][:], ST["ssz"][:], 1.0 / C, ST["tmp"][:],
                ALU.mult, ALU.subtract), vv)
            wait("scalar", var_v)
            run("scalar", lambda e: e.activation(ST["r_ln"][:], ST["tmp"][:],
                                                 AF.Ln, bias=EPS_LN))
            rln_s = inc("scalar", lambda e: e.activation(ST["r_ln"][:], ST["r_ln"][:],
                                                         AF.Exp, scale=-0.5), sv)
            wait("vector", rln_s)
            fin_v = []
            for t in range(TT):
                fv = inc("vector", lambda e, a=xz[:, t * C:(t + 1) * C],
                         mm=ST["m_ln"][:, t:t + 1], r=ST["r_ln"][:, t:t + 1]:
                         e.tensor_scalar(a, a, mm, r, ALU.subtract, ALU.mult), vv)
                fin_v.append(fv)
            nst = 0
            for g in range(QG):
                wait("sync", fin_v[(g + 1) * GT - 1])
                dma("sync", stm,
                    out_d[g * GT * P:(g + 1) * GT * P, :].rearrange("(t p) c -> p t c", p=P),
                    xz[:, g * GT * C:(g + 1) * GT * C].rearrange("p (t c) -> p t c", t=GT))
                nst += 16
            wait("sync", (stm, nst))

        if upto < 5:
            p4ctx.close()
        if upto < 6:
            if upto == 1:
                wait("vector", (vv, vv.n))
                wait("vector", (sv, sv.n))
                wait("vector", (tv, tv.n))
                for ii, nm in enumerate(["ssq", "r_x", "amax", "phi", "ef"]):
                    run("vector", lambda e, d=xz[:, ii * 16:ii * 16 + 16], s=ST[nm][:]:
                        e.tensor_copy(d, s))
                run("vector", lambda e: e.tensor_copy(xz[:, 128:256], nT_c[0][:]))
                run("vector", lambda e: e.tensor_copy(xz[:, 256:384], nT_c[7][:]))
                run("vector", lambda e: e.tensor_copy(xz[:, 1024:2048], ptm[:]))
                inc("vector", lambda e: e.tensor_copy(xz[:, 384:387], SW["q"][:, 0:3]), vv)
            if upto == 3:
                wait("vector", (vv, vv.n))
                wait("vector", (sv, sv.n))
                for i in range(8):
                    run("vector", lambda e, d=xz[:, i * P:(i + 1) * P], s=k_c[i][3][:]:
                        e.tensor_copy(d, s))
                for i in range(8):
                    inc("vector", lambda e, d=xz[:, C + i * P:C + (i + 1) * P],
                        s=v_c[3][i][:, 0:64].to_broadcast([P, 128]) if False else v_c[3][i][:],
                        dd=xz[:, C + i * P:C + i * P + 65]: e.tensor_copy(dd, s), vv)
            if upto == 5:
                wait("vector", (vv, vv.n))
                wait("vector", (sv, sv.n))
                wait("vector", (tv, tv.n))
                run("vector", lambda e: e.tensor_copy(xz[:, 0:128], smask[:]))
                run("vector", lambda e: e.tensor_copy(xz[:, 128:256], lg[:]))
                run("vector", lambda e: e.tensor_copy(xz[:, 256:272], ST["sumex"][:]))
                run("vector", lambda e: e.tensor_copy(xz[:, 272:288], ST["nmx"][:]))
                run("vector", lambda e: e.tensor_copy(xz[:, 288:304], ST["score"][:]))
                run("vector", lambda e: e.tensor_copy(xz[:, C:C + 128], h_c[0][0][:]))
                run("vector", lambda e: e.tensor_copy(xz[:, C + 128:C + 256], h_c[7][0][:]))
                run("vector", lambda e: e.tensor_copy(xz[:, C + 256:C + 384],
                                                      h_c[0][1][:]))
                run("vector", lambda e: e.tensor_copy(xz[:, 2 * C:3 * C], u_b[:]))
                inc("vector", lambda e: e.tensor_copy(xz[:, 304:305],
                                                      recb[:, 0:1]), vv)
            if upto == 4:
                wait("vector", (vv, vv.n))
                wait("vector", (sv, sv.n))
                run("vector", lambda e: e.tensor_copy(xz[:, 0:C], y_g[:, 0:C]))
                for ii, nm in enumerate(["ssy", "r_y", "amax", "phi", "ef", "s1",
                                         "ssz", "m_ln", "r_ln"]):
                    run("vector", lambda e, d=xz[:, C + ii * 16:C + ii * 16 + 16],
                        s=ST[nm][:]: e.tensor_copy(d, s))
                inc("vector", lambda e: e.tensor_copy(xz[:, C + 160:C + 163],
                                                      SW["o"][:, 0:3]), vv)
            nst = 0
            for g in range(QG):
                wait("sync", (vv, vv.n))
                wait("sync", (sv, sv.n))
                wait("sync", (tv, tv.n))
                dma("sync", stm,
                    out_d[g * GT * P:(g + 1) * GT * P, :].rearrange("(t p) c -> p t c", p=P),
                    xz[:, g * GT * C:(g + 1) * GT * C].rearrange("p (t c) -> p t c", t=GT))
                nst += 16
            wait("sync", (stm, nst))

        blk = Eb(nc.Block())

        def replay(e, items):
            for it in items:
                if it[0] == "op":
                    it[1](e)
                elif it[0] == "wait":
                    e.wait_ge(it[1].h, it[2])
                else:
                    it[1](e).then_inc(it[2].h, it[3])

        @blk.sync
        def _(e):
            replay(e, pl.q["sync"])

        @blk.gpsimd
        def _(e):
            replay(e, pl.q["gpsimd"])

        @blk.tensor
        def _(e):
            replay(e, pl.q["tensor"])

        @blk.vector
        def _(e):
            replay(e, pl.q["vector"])

        @blk.scalar
        def _(e):
            replay(e, pl.q["scalar"])

    return nc


_NC_CACHE = {}


def _get_nc():
    if "nc" not in _NC_CACHE:
        import os
        _NC_CACHE["nc"] = build(upto=int(os.environ.get("KUPTO", "6")),
                                sub=int(os.environ.get("KSUB", "6")))
    return _NC_CACHE["nc"]


def kernel(L0_attn_x, L0_attn_wq, L0_attn_wk, L0_attn_wv, L0_attn_wo,
           L0_attn_gq, L0_attn_gk, L0_attn_gv, L0_attn_go,
           L0_ffn_rms_w, L0_ffn_gate_w, L0_ffn_W1, L0_ffn_b1, L0_ffn_W2, L0_ffn_b2):
    import ml_dtypes
    x = np.ascontiguousarray(np.asarray(L0_attn_x, dtype=np.float32))
    B = x.shape[0]

    def asf(a):
        return np.ascontiguousarray(np.asarray(a, dtype=np.float32))

    wT = {"q": asf(L0_attn_wq).T.copy(), "k": asf(L0_attn_wk).T.copy(),
          "v": asf(L0_attn_wv).T.copy(), "o": asf(L0_attn_wo).T.copy()}
    g_rep = {m: np.ascontiguousarray(np.broadcast_to(asf(g)[None, :], (P, C)))
             for m, g in (("q", L0_attn_gq), ("k", L0_attn_gk),
                          ("v", L0_attn_gv), ("o", L0_attn_go))}
    rmsw = np.ascontiguousarray(np.broadcast_to(asf(L0_ffn_rms_w)[None, :], (P, C)))
    gwT = asf(L0_ffn_gate_w).T.copy()
    W1 = asf(L0_ffn_W1)
    W2 = asf(L0_ffn_W2)
    b1r = np.ascontiguousarray(asf(L0_ffn_b1).reshape(NE, FD // P, P).transpose(0, 2, 1))
    b2 = asf(L0_ffn_b2)
    identb = np.eye(P, dtype=ml_dtypes.bfloat16)
    ident8 = np.eye(8, dtype=ml_dtypes.bfloat16)
    triu_m = np.triu(np.ones((P, P), np.float32)).astype(ml_dtypes.bfloat16)
    oh8 = np.zeros((NE, 8, P), dtype=ml_dtypes.bfloat16)
    for e_i in range(NE):
        oh8[e_i, e_i, :] = 1.0
    iota8 = np.ascontiguousarray(np.broadcast_to(
        np.arange(NE, dtype=np.float32)[None, :], (P, NE)))
    cvals = np.ascontiguousarray(np.broadcast_to(np.array(
        [EPS_RMS, EPS_LN, -0.5, 0.125, 1.0 / C, EPS_Q, 1.0 / 127.0, 1.0 / (C * C)],
        dtype=np.float32)[None, :], (P, 8)))

    shared = dict(identb=identb, ident8=ident8, triu=triu_m,
                  oh8=oh8, iota8=iota8, cvals=cvals,
                  gwT=gwT, W1=W1, W2=W2, b1r=b1r, b2=b2, rmsw=rmsw)
    for m in "qkvo":
        shared[f"w{m}T"] = wT[m]
        shared[f"g{m}"] = g_rep[m]

    nc = _get_nc()
    in_maps = [dict(shared, x=x[b]) for b in range(B)]
    res = run_bass_kernel_spmd(nc, in_maps, core_ids=list(range(B)))
    out = np.stack([np.asarray(res.results[b]["out"]) for b in range(B)], axis=0)
    return out.astype(np.float32)
